# version 06
# baseline (speedup 1.0000x reference)
import math
import numpy as np
import ml_dtypes
from contextlib import ExitStack

import concourse.bass as bass
import concourse.mybir as mybir
from concourse.bass_utils import run_bass_kernel_spmd

F32 = mybir.dt.float32
BF16 = mybir.dt.bfloat16
AF = mybir.ActivationFunctionType
OP = mybir.AluOpType
AX = mybir.AxisListType

NCORES = 8
B_TOT, T, D = 16, 2048, 1024
BPC = B_TOT // NCORES
H, HD, HV = 8, 64, 128
NE, EH, NL = 8, 4096, 4
NBLK = 2
NSUB = 2
NTC = 4
CH = 128
EPS = 1e-5
DSC = 1.0 / math.sqrt(HD)

BARRIER = False


def build_consts():
    gam = 1.0 - np.exp2(-5.0 - np.arange(H, dtype=np.float64))
    invf = 1.0 / (10000.0 ** (np.arange(0, HD, 2, dtype=np.float64) / HD))
    t_ = np.arange(T, dtype=np.float64)
    d_ = np.arange(128) % HD
    th = t_[None, :] * invf[d_ % 32][:, None]
    rope_cos = np.cos(th).astype(np.float32)
    rope_sin = np.sin(th).astype(np.float32)
    j = np.arange(CH)
    rel = j[None, :] - j[:, None]
    masks = np.zeros((CH, H, CH), np.float32)
    for h in range(H):
        masks[:, h, :] = (np.where(rel >= 0, gam[h] ** np.maximum(rel, 0), 0.0) * DSC).astype(np.float32)
    qsc = np.zeros((128, H // 2, CH), np.float32)
    for qt in range(H // 2):
        for par in range(2):
            g = gam[2 * qt + par]
            qsc[par * 64:(par + 1) * 64, qt, :] = (g ** (np.arange(CH) + 1.0) * DSC)[None, :]
    kvec = np.zeros((128, H), np.float32)
    for h in range(H):
        kvec[:, h] = gam[h] ** (127.0 - np.arange(128))
    g128c = np.zeros((128, H // 2), np.float32)
    for qt in range(H // 2):
        g128c[0:64, qt] = gam[2 * qt] ** 128
        g128c[64:128, qt] = gam[2 * qt + 1] ** 128
    p2 = np.zeros((128, 128), np.float32)
    for b in range(2):
        for dd in range(32):
            p2[b * 64 + dd, b * 64 + dd + 32] = -1.0
            p2[b * 64 + dd + 32, b * 64 + dd] = 1.0
    p2t = np.asarray(p2.T, dtype=ml_dtypes.bfloat16)
    masks = np.asarray(masks, dtype=ml_dtypes.bfloat16)
    qsc = np.asarray(qsc, dtype=ml_dtypes.bfloat16)
    ident = np.asarray(np.eye(128), dtype=ml_dtypes.bfloat16)
    onehot = np.zeros((8, 1024), np.float32)
    for e in range(8):
        onehot[e, e * 128:(e + 1) * 128] = 1.0
    onehot = np.asarray(onehot, dtype=ml_dtypes.bfloat16)
    return dict(rope_cos=rope_cos, rope_sin=rope_sin, masks=masks, qsc=qsc,
                kvec=kvec, g128c=g128c, p2t=p2t, ident=ident, onehot=onehot)


class Sem:
    def __init__(self, handle, name):
        self.h = handle
        self.name = name
        self.n = 0

    def inc(self, k=1):
        self.n += k
        return self.n


class Prog:
    ENGS = ("sync", "gpsimd", "vector", "scalar", "tensor")
    CTAG = {"tensor": "TE", "vector": "VE", "scalar": "SE", "gpsimd": "PO"}
    QTAG = {"sync": "SQ", "gpsimd": "GQ"}

    def __init__(self, nc, es):
        self.q = {e: [] for e in self.ENGS}
        self.sems = {n: Sem(es.enter_context(nc.semaphore(n)), n)
                     for n in ("TE", "VE", "SE", "PO", "SQ", "GQ")}
        self.hw = {}
        self.res = {}

    def _wait(self, eng, semname, val):
        if val <= 0:
            return
        key = (eng, semname)
        if self.hw.get(key, 0) >= val:
            return
        self.hw[key] = val
        s = self.sems[semname].h
        self.q[eng].append(lambda e, s=s, v=val: e.wait_ge(s, v))

    def _edges(self, eng, tag, reads, writes):
        for k in reads:
            rec = self.res.get(k)
            if rec:
                for sn, v in rec["w"].items():
                    self._wait(eng, sn, v)
        for k in writes:
            rec = self.res.get(k)
            if rec:
                for sn, v in rec["w"].items():
                    if sn != tag:
                        self._wait(eng, sn, v)
                for sn, v in rec["r"].items():
                    if sn != tag:
                        self._wait(eng, sn, v)

    def _record(self, tag, val, reads, writes):
        for k in writes:
            rec = self.res.setdefault(k, {"w": {}, "r": {}})
            if rec["w"].get(tag, 0) < val:
                rec["w"][tag] = val
        for k in reads:
            rec = self.res.setdefault(k, {"w": {}, "r": {}})
            if rec["r"].get(tag, 0) < val:
                rec["r"][tag] = val

    def op(self, eng, fn, reads=(), writes=()):
        tag = self.CTAG[eng]
        self._edges(eng, tag, reads, writes)
        sem = self.sems[tag]
        v = sem.inc(1)
        self.q[eng].append(lambda e, f=fn, s=sem.h: f(e).then_inc(s, 1))
        self._record(tag, v, reads, writes)
        return (tag, v)

    def dma(self, qeng, out, in_, reads=(), writes=(), slow=False):
        tag = self.QTAG[qeng]
        self._edges(qeng, tag, reads, writes)
        sem = self.sems[tag]
        v = sem.inc(16)
        if slow:
            self.q[qeng].append(lambda e, o=out, i=in_, s=sem.h: e.dma_start(
                out=o, in_=i, allow_slow_non_contiguous=True).then_inc(s, 16))
        else:
            self.q[qeng].append(lambda e, o=out, i=in_, s=sem.h: e.dma_start(
                out=o, in_=i).then_inc(s, 16))
        self._record(tag, v, reads, writes)
        return (tag, v)

    def barrier(self):
        snap = [(n, s.n) for n, s in self.sems.items()]
        for eng in self.ENGS:
            for n, v in snap:
                self._wait(eng, n, v)

    def maybe_barrier(self):
        if BARRIER:
            self.barrier()


def build_nc(dbg=False):
    consts = build_consts()

    nc = bass.Bass("TRN2", target_bir_lowering=False)
    d_x = nc.dram_tensor("x", [BPC, T, D], F32, kind="ExternalInput").ap()
    d_wq = nc.dram_tensor("wq", [H * HD, D], F32, kind="ExternalInput").ap()
    d_wk = nc.dram_tensor("wk", [H * HD, D], F32, kind="ExternalInput").ap()
    d_wv = nc.dram_tensor("wv", [H * HV, D], F32, kind="ExternalInput").ap()
    d_wg = nc.dram_tensor("wg", [H * HV, D], F32, kind="ExternalInput").ap()
    d_wo = nc.dram_tensor("wo", [D, H * HV], F32, kind="ExternalInput").ap()
    d_gnw = nc.dram_tensor("gnw", [H * HV], F32, kind="ExternalInput").ap()
    d_gnb = nc.dram_tensor("gnb", [H * HV], F32, kind="ExternalInput").ap()
    d_rw = nc.dram_tensor("rw", [NE, D], F32, kind="ExternalInput").ap()
    d_rb = nc.dram_tensor("rb", [NE], F32, kind="ExternalInput").ap()
    d_lw = nc.dram_tensor("lw", [NL, D], F32, kind="ExternalInput").ap()
    d_lb = nc.dram_tensor("lb", [NL], F32, kind="ExternalInput").ap()
    d_w1 = nc.dram_tensor("w1", [NE, D, EH], F32, kind="ExternalInput").ap()
    d_b1 = nc.dram_tensor("b1", [NE, EH], F32, kind="ExternalInput").ap()
    d_w2 = nc.dram_tensor("w2", [NE, EH, D], F32, kind="ExternalInput").ap()
    d_b2 = nc.dram_tensor("b2", [NE, D], F32, kind="ExternalInput").ap()
    d_cos = nc.dram_tensor("rope_cos", [128, T], F32, kind="ExternalInput").ap()
    d_sin = nc.dram_tensor("rope_sin", [128, T], F32, kind="ExternalInput").ap()
    d_msk = nc.dram_tensor("masks", [CH, H, CH], BF16, kind="ExternalInput").ap()
    d_qsc = nc.dram_tensor("qsc", [128, H // 2, CH], BF16, kind="ExternalInput").ap()
    d_kvec = nc.dram_tensor("kvec", [128, H], F32, kind="ExternalInput").ap()
    d_g128 = nc.dram_tensor("g128c", [128, H // 2], F32, kind="ExternalInput").ap()
    d_p2t = nc.dram_tensor("p2t", [128, 128], BF16, kind="ExternalInput").ap()
    d_id = nc.dram_tensor("ident", [128, 128], BF16, kind="ExternalInput").ap()
    d_oh = nc.dram_tensor("onehot", [8, 1024], BF16, kind="ExternalInput").ap()
    d_y = nc.dram_tensor("y", [BPC, T, D], F32, kind="ExternalOutput").ap()
    if dbg:
        d_tap = {
            "xf0": nc.dram_tensor("t_xf0", [128, 8, 1024], BF16, kind="ExternalOutput").ap(),
            "wblk0": nc.dram_tensor("t_wblk0", [128, 8, 8], F32, kind="ExternalOutput").ap(),
            "qf0": nc.dram_tensor("t_qf0", [128, 4, 512], BF16, kind="ExternalOutput").ap(),
            "kf0": nc.dram_tensor("t_kf0", [128, 4, 512], BF16, kind="ExternalOutput").ap(),
            "vt0": nc.dram_tensor("t_vt0", [128, 4, 1024], BF16, kind="ExternalOutput").ap(),
            "gt0": nc.dram_tensor("t_gt0", [128, 4, 1024], BF16, kind="ExternalOutput").ap(),
            "ogt0": nc.dram_tensor("t_ogt0", [128, 4, 1024], BF16, kind="ExternalOutput").ap(),
            "oattn": nc.dram_tensor("t_oattn", [128, 8, 1024], F32, kind="ExternalOutput").ap(),
            "hsb0": nc.dram_tensor("t_hsb0", [128, 8, 1024], BF16, kind="ExternalOutput").ap(),
            "ofin": nc.dram_tensor("t_ofin", [128, 8, 1024], F32, kind="ExternalOutput").ap(),
        }
    d_wT = nc.dram_tensor("wT_scr", [128, 8, 4096], BF16, kind="Internal").ap()

    es = ExitStack()

    def sb(name, shape, dt):
        return es.enter_context(nc.sbuf_tensor(name, shape, dt))

    def ps(name, shape, dt):
        return es.enter_context(nc.psum_tensor(name, shape, dt))

    Xf = sb("Xf", [128, 8, 1024], BF16)
    Oacc = sb("Oacc", [128, 8, 1024], F32)
    wblk = sb("wblk", [128, 8, NE], F32)
    msk_s = sb("msk_s", [128, H, CH], BF16)
    qsc_s = sb("qsc_s", [128, H // 2, CH], BF16)
    kvec_s = sb("kvec_s", [128, H], F32)
    g128_s = sb("g128_s", [128, H // 2], F32)
    p2t_s = sb("p2t_s", [128, 128], BF16)
    id_s = sb("id_s", [128, 128], BF16)
    gnw_b = sb("gnw_b", [128, H * HV], BF16)
    gnb_b = sb("gnb_b", [128, H * HV], BF16)
    rlwT = sb("rlwT", [128, 8, 16], BF16)
    rlb_b = sb("rlb_b", [128, 16], F32)
    b1_s = sb("b1_s", [128, NE * 32], F32)
    b2_s = sb("b2_s", [8, 1024], BF16)
    oh_s = sb("oh_s", [8, 1024], BF16)
    eps_v = sb("eps_v", [128, 1], F32)
    cos_s = sb("cos_s", [128, 512], F32)
    sin_s = sb("sin_s", [128, 512], F32)
    S_sb = sb("S_sb", [128, H // 2, 128], F32)
    S_bf = sb("S_bf", [128, H // 2, 2, 128], BF16)
    x_t = sb("x_t", [128, 2, 1024], F32)
    xn_s = sb("xn_s", [128, 2, 1024], BF16)
    lst = sb("lst", [128, 2, 16], F32)
    rt_sc = sb("rt_sc", [128, 2, 64], F32)
    Qraw = sb("Qraw", [128, 2, 512], BF16)
    rot1 = sb("rot1", [128, 512], F32)
    rot2 = sb("rot2", [128, 512], F32)
    Q_F = sb("Q_F", [128, 4, 512], BF16)
    K_F = sb("K_F", [128, 4, 512], BF16)
    Qt_a = sb("Qt_a", [128, 4, 512], BF16)
    Kt_a = sb("Kt_a", [128, 4, 4, 128], BF16)
    V_T = sb("V_T", [128, 4, 1024], BF16)
    G_T = sb("G_T", [128, 4, 1024], BF16)
    Og_T = sb("Og_T", [128, 4, 1024], BF16)
    Og_F = sb("Og_F", [128, 2, 8, 128], BF16)
    ScT_s = sb("ScT_s", [128, 2, 128], BF16)
    NOR = 6
    Oraw = sb("Oraw", [128, NOR, 128], F32)
    st6r = sb("st6r", [128, NOR, 6], F32)
    mvr = sb("mvr", [128, NOR, 2], F32)
    negm_r = sb("negm_r", [128, NOR, 1], F32)
    rstd_r = sb("rstd_r", [128, NOR, 1], F32)
    bcol_r = sb("bcol_r", [128, NOR, 1], F32)
    tg1 = sb("tg1", [128, 3, 128], BF16)
    tg2 = sb("tg2", [128, 3, 128], BF16)
    tg3 = sb("tg3", [128, 3, 128], BF16)
    wTld = sb("wTld", [128, 2, 8, 512], BF16)

    pb = [ps(f"pb{i}", [128, 512], F32) for i in range(6)]
    pt = [ps(f"pt{i}", [128, 1024], BF16) for i in range(2)]

    P = Prog(nc, es)

    nums = sorted(s.h.num for s in P.sems.values())
    assert nums[-1] - nums[0] == len(nums) - 1, nums
    nc.all_engine_barrier()
    nc.gpsimd.dma_reset(range(nums[0], nums[-1] + 1))
    nc.gpsimd.sem_clear(range(nums[0], nums[-1] + 1))
    nc.all_engine_barrier()

    def TE(fn, reads=(), writes=()):
        return P.op("tensor", fn, reads, writes)

    def VE(fn, reads=(), writes=()):
        return P.op("vector", fn, reads, writes)

    def SE(fn, reads=(), writes=()):
        return P.op("scalar", fn, reads, writes)

    def PO(fn, reads=(), writes=()):
        return P.op("gpsimd", fn, reads, writes)

    def mm(out_ap, lhsT, rhs, start, stop, reads=(), writes=()):
        return TE(lambda t, o=out_ap, l=lhsT, r=rhs, st=start, sp=stop:
                  t.matmul(o, l, r, start=st, stop=sp), reads, writes)

    def tr(out_ap, in_ap, reads=(), writes=(), ident_ap=None):
        ia = ident_ap if ident_ap is not None else id_s[:, :]
        return TE(lambda t, o=out_ap, i=in_ap, ia=ia: t.transpose(o, i, ia),
                  reads, writes)

    for out, in_, key in [
            (msk_s[:, :, :], d_msk[:, :, :], "msk"),
            (qsc_s[:, :, :], d_qsc[:, :, :], "qsc"),
            (kvec_s[:, :], d_kvec[:, :], "kvec"),
            (g128_s[:, :], d_g128[:, :], "g128"),
            (p2t_s[:, :], d_p2t[:, :], "p2t"),
            (id_s[:, :], d_id[:, :], "id"),
            (rlb_b[0:1, 0:8], d_rb[None, :], "rlb"),
            (rlb_b[0:1, 8:12], d_lb[None, :], "rlb"),
            (oh_s[:, :], d_oh[:, :], "oh")]:
        P.dma("sync", out, in_, writes=[key])
    P.dma("sync", b1_s[:, :], d_b1.rearrange("e (hcg p) -> p (e hcg)", p=128),
          writes=["b1"], slow=True)
    P.dma("gpsimd", b2_s[:, :], d_b2[:, :], writes=["b2s"])

    VE(lambda e: e.memset(eps_v[:, :], EPS), writes=["eps"])
    VE(lambda e: e.memset(S_sb.rearrange("p a b -> p (a b)"), 0.0), writes=["Ssb0", "Ssb1", "Ssb2", "Ssb3"])
    VE(lambda e: e.memset(S_bf.rearrange("p a b c -> p (a b c)"), 0.0), writes=["Sbf0", "Sbf1", "Sbf2", "Sbf3"])
    VE(lambda e: e.memset(rlb_b[0:1, 12:16], 0.0), writes=["rlb"])

    with nc.sbuf_tensor("wtmp", [128, 4, 1024], BF16) as wtmp, \
         nc.sbuf_tensor("wTst", [128, 8, 512], BF16) as wTst, \
         nc.sbuf_tensor("gn_f32", [1, 2, H * HV], F32) as gn_f32, \
         nc.sbuf_tensor("onescol", [1, 128], F32) as onescol, \
         nc.sbuf_tensor("rl_s", [12, 1024], F32) as rl_s, \
         nc.sbuf_tensor("rl_bf", [12, 1024], BF16) as rl_bf:
        P.dma("sync", gn_f32[0:1, 0, :], d_gnw[None, :], writes=["gnf"])
        P.dma("sync", gn_f32[0:1, 1, :], d_gnb[None, :], writes=["gnf"])
        VE(lambda e: e.memset(onescol[:, :], 1.0), writes=["ones"])
        for gsel in range(2):
            dstb = gnw_b if gsel == 0 else gnb_b
            dkey = "gnw" if gsel == 0 else "gnb"
            for half in range(2):
                b = (gsel * 2 + half) % 2
                mm(pb[b][:, :], onescol[:, :],
                   gn_f32[0:1, gsel, half * 512:(half + 1) * 512],
                   True, True, reads=["ones", "gnf"], writes=[("pb", b)])
                VE(lambda e, dstb=dstb, half=half, b=b: e.tensor_copy(
                    dstb[:, half * 512:(half + 1) * 512], pb[b][:, :]),
                    reads=[("pb", b)], writes=[dkey])
        mm(pb[2][:, 0:16], onescol[:, :], rlb_b[0:1, :], True, True,
           reads=["ones", "rlb"], writes=[("pb", 2)])
        VE(lambda e: e.tensor_copy(rlb_b[:, :], pb[2][:, 0:16]),
           reads=[("pb", 2)], writes=["rlb"])
        P.dma("sync", rl_s[0:8, :], d_rw[:, :], writes=["rls"])
        P.dma("sync", rl_s[8:12, :], d_lw[:, :], writes=["rls"])
        VE(lambda e: e.tensor_copy(rl_bf[:, :], rl_s[:, :]),
           reads=["rls"], writes=["rlbf"])
        for fc in range(8):
            b = fc % 2
            tr(pt[b][:, 0:12], rl_bf[:, fc * 128:(fc + 1) * 128],
               reads=["rlbf", "id"], writes=[("pt", b)],
               ident_ap=id_s[0:12, 0:12])
            SE(lambda e, fc=fc, b=b: e.activation(rlwT[:, fc, 0:12], pt[b][:, 0:12], AF.Copy),
               reads=[("pt", b)], writes=["rlwT"])
        VE(lambda e: e.memset(rlwT[:, :, 12:16], 0.0), writes=["rlwT"])

        specs = [(d_wq, 512, 0), (d_wk, 512, 512), (d_wv, 1024, 1024),
                 (d_wg, 1024, 2048), (d_wo, 1024, 3072)]
        for w_d, rows, coff in specs:
            for hp in range(rows // 512):
                P.dma("gpsimd", wtmp[:, :, :],
                      w_d[hp * 512:(hp + 1) * 512, :].rearrange("(oc p) f -> p oc f", p=128),
                      writes=["wtmp"])
                for oc in range(4):
                    for fc in range(8):
                        b = (oc * 8 + fc) % 2
                        tr(pt[b][:, 0:128], wtmp[:, oc, fc * 128:(fc + 1) * 128],
                           reads=["wtmp", "id"], writes=[("pt", b)])
                        SE(lambda e, fc=fc, oc=oc, b=b:
                           e.activation(wTst[:, fc, oc * 128:(oc + 1) * 128], pt[b][:, 0:128], AF.Copy),
                           reads=[("pt", b)], writes=["wTst"])
                P.dma("sync", d_wT[:, :, coff + hp * 512:coff + (hp + 1) * 512],
                      wTst[:, :, :], reads=["wTst"], writes=["dwT"])
        P.barrier()

    w1buf = sb("w1buf", [128, 8, 1024], BF16)
    w2buf = sb("w2buf", [128, 8, 1024], BF16)
    H_sb = sb("H_sb", [128, 8, 1024], BF16)

    def load_w1(e_i, rq):
        return P.dma("gpsimd", w1buf[:, :, :],
                      d_w1[e_i, :, rq * 1024:(rq + 1) * 1024].rearrange("(fc p) h -> p fc h", p=128),
                      writes=["w1"])

    def load_w2(e_i, rq):
        return P.dma("gpsimd", w2buf[:, :, :],
                      d_w2[e_i, rq * 1024:(rq + 1) * 1024, :].rearrange("(hc p) d -> p hc d", p=128),
                      writes=["w2"])

    for bel in range(BPC):
        for blk in range(NBLK):
            blk_t0 = blk * 1024
            load_w1(0, 0)

            for sub in range(NSUB):
                sub_t0 = blk_t0 + sub * 512
                P.dma("sync", cos_s[:, :], d_cos[:, sub_t0:sub_t0 + 512], writes=["cos"])
                P.dma("sync", sin_s[:, :], d_sin[:, sub_t0:sub_t0 + 512], writes=["sin"])
                P.dma("sync", wTld[:, 0, :, :], d_wT[:, :, 0:512], reads=["dwT"], writes=[("wT", 0)])
                P.dma("sync", wTld[:, 1, :, :], d_wT[:, :, 512:1024], reads=["dwT"], writes=[("wT", 1)])

                for tc in range(NTC):
                    par = tc % 2
                    toff = sub_t0 + tc * 128
                    xk = ("xt", par)
                    lk = ("lst", par)
                    P.dma("sync", x_t[:, par, :], d_x[bel, toff:toff + 128, :], writes=[xk])
                    VE(lambda e, par=par: e.bn_stats(lst[:, par, 0:6], x_t[:, par, 0:512]),
                       reads=[xk], writes=[lk])
                    VE(lambda e, par=par: e.bn_stats(lst[:, par, 6:12], x_t[:, par, 512:1024]),
                       reads=[xk], writes=[lk])
                    VE(lambda e, par=par: e.bn_aggr(lst[:, par, 12:14], lst[:, par, 0:12]),
                       reads=[lk], writes=[lk])
                    SE(lambda e, par=par: e.activation(lst[:, par, 14:15], lst[:, par, 12:13],
                                                       AF.Copy, scale=-1.0),
                       reads=[lk], writes=[lk])
                    SE(lambda e, par=par: e.activation(lst[:, par, 15:16], lst[:, par, 13:14],
                                                       AF.Ln, bias=eps_v[:, :]),
                       reads=[lk, "eps"], writes=[lk])
                    SE(lambda e, par=par: e.activation(lst[:, par, 15:16], lst[:, par, 15:16],
                                                       AF.Exp, scale=-0.5),
                       reads=[lk], writes=[lk])
                    VE(lambda e, par=par: e.tensor_scalar(
                        xn_s[:, par, :], x_t[:, par, :], lst[:, par, 14:15],
                        lst[:, par, 15:16], OP.add, OP.mult),
                       reads=[xk, lk], writes=[("xn", par)])
                    for fc in range(8):
                        b = fc % 2
                        tr(pt[b][:, 0:128], xn_s[:, par, fc * 128:(fc + 1) * 128],
                           reads=[("xn", par), "id"], writes=[("pt", b)])
                        SE(lambda e, fc=fc, b=b, off=sub * 512 + tc * 128:
                           e.activation(Xf[:, fc, off:off + 128], pt[b][:, 0:128], AF.Copy),
                           reads=[("pt", b)], writes=["Xf"])
                P.maybe_barrier()

                for tc in range(NTC):
                    tcg = sub * 4 + tc
                    sl = tc % 2
                    bkey = ("pb", 4 + sl)
                    pr = pb[4 + sl]
                    rk = ("rt", sl)
                    r = rt_sc[:, sl, :]
                    for fc in range(8):
                        mm(pr[:, 0:16], Xf[:, fc, tcg * 128:tcg * 128 + 128],
                           rlwT[:, fc, :], start=(fc == 0), stop=(fc == 7),
                           reads=["Xf", "rlwT"], writes=[bkey])
                    VE(lambda e, pr=pr, r=r: e.tensor_tensor(r[:, 0:16], pr[:, 0:16], rlb_b[:, :], OP.add),
                       reads=[bkey, "rlb"], writes=[rk])
                    VE(lambda e, r=r: e.reduce_max(r[:, 44:45], r[:, 8:12], axis=AX.X, negate=True),
                       reads=[rk], writes=[rk])
                    SE(lambda e, r=r: e.activation(r[:, 16:20], r[:, 8:12], AF.Exp, bias=r[:, 44:45]),
                       reads=[rk], writes=[rk])
                    VE(lambda e, r=r: e.reduce_sum(r[:, 45:46], r[:, 16:20], axis=AX.X),
                       reads=[rk], writes=[rk])
                    VE(lambda e, r=r: e.reciprocal(r[:, 46:47], r[:, 45:46]),
                       reads=[rk], writes=[rk])
                    VE(lambda e, r=r: e.tensor_scalar(r[:, 20:24], r[:, 16:20], r[:, 46:47], None, OP.mult),
                       reads=[rk], writes=[rk])
                    VE(lambda e, r=r: e.tensor_tensor(r[:, 24:27], r[:, 20:23], r[:, 21:24], OP.add),
                       reads=[rk], writes=[rk])
                    VE(lambda e, r=r: e.tensor_copy(r[:, 27:28], r[:, 23:24]),
                       reads=[rk], writes=[rk])
                    VE(lambda e, r=r: e.scalar_tensor_tensor(r[:, 28:32], r[:, 24:28], 0.1, r[:, 0:4], OP.mult, OP.add),
                       reads=[rk], writes=[rk])
                    VE(lambda e, r=r: e.scalar_tensor_tensor(r[:, 32:36], r[:, 24:28], 0.1, r[:, 4:8], OP.mult, OP.add),
                       reads=[rk], writes=[rk])
                    VE(lambda e, r=r: e.max(r[:, 36:44], r[:, 28:36]),
                       reads=[rk], writes=[rk])
                    VE(lambda e, r=r: e.tensor_scalar(r[:, 44:45], r[:, 36:37], -1.0, None, OP.mult),
                       reads=[rk], writes=[rk])
                    SE(lambda e, r=r: e.activation(r[:, 48:56], r[:, 28:36], AF.Exp, bias=r[:, 44:45]),
                       reads=[rk], writes=[rk])
                    SE(lambda e, r=r: e.activation(r[:, 45:46], r[:, 37:38], AF.Exp, bias=r[:, 44:45]),
                       reads=[rk], writes=[rk])
                    VE(lambda e, r=r: e.tensor_scalar(r[:, 46:47], r[:, 45:46], 1.0, None, OP.add),
                       reads=[rk], writes=[rk])
                    VE(lambda e, r=r: e.reciprocal(r[:, 47:48], r[:, 46:47]),
                       reads=[rk], writes=[rk])
                    VE(lambda e, r=r: e.tensor_scalar(r[:, 56:64], r[:, 28:36], r[:, 37:38], None, OP.is_ge),
                       reads=[rk], writes=[rk])
                    VE(lambda e, r=r, tcg=tcg: e.scalar_tensor_tensor(
                        wblk[:, tcg, :], r[:, 48:56], r[:, 47:48], r[:, 56:64], OP.mult, OP.mult),
                       reads=[rk], writes=["wblk"])
                P.maybe_barrier()

                for tile in range(8):
                    bsel = tile % 2
                    bkey = ("pb", bsel)
                    pq = pb[bsel]
                    wsl = tile // 4
                    for fc in range(8):
                        mm(pq[:, :], wTld[:, wsl, fc, (tile % 4) * 128:(tile % 4 + 1) * 128],
                           Xf[:, fc, sub * 512:sub * 512 + 512],
                           start=(fc == 0), stop=(fc == 7),
                           reads=[("wT", wsl), "Xf"], writes=[bkey])
                    qsl = tile % 2
                    SE(lambda e, pq=pq, qsl=qsl: e.activation(Qraw[:, qsl, :], pq[:, :], AF.Copy),
                       reads=[bkey], writes=[("Qraw", qsl)])
                    b2k = ("pb", 2 + bsel)
                    prot = pb[2 + bsel]
                    mm(prot[:, :], p2t_s[:, :], Qraw[:, qsl, :], True, True,
                       reads=["p2t", ("Qraw", qsl)], writes=[b2k])
                    VE(lambda e, qsl=qsl: e.tensor_tensor(rot1[:, :], Qraw[:, qsl, :], cos_s[:, :], OP.mult),
                       reads=[("Qraw", qsl), "cos"], writes=["rot1"])
                    VE(lambda e, prot=prot: e.tensor_tensor(rot2[:, :], prot[:, :], sin_s[:, :], OP.mult),
                       reads=[b2k, "sin"], writes=["rot2"])
                    dst = Q_F if tile < 4 else K_F
                    dk = "QF" if tile < 4 else "KF"
                    VE(lambda e, dst=dst, dt_i=tile % 4: e.tensor_tensor(
                        dst[:, dt_i, :], rot1[:, :], rot2[:, :], OP.add),
                       reads=["rot1", "rot2"], writes=[dk])
                for qt in range(4):
                    for c in range(4):
                        VE(lambda e, qt=qt, c=c: e.tensor_tensor(
                            Qt_a[:, qt, c * 128:(c + 1) * 128],
                            Q_F[:, qt, c * 128:(c + 1) * 128],
                            qsc_s[:, qt, :], OP.mult),
                           reads=["QF", "qsc"], writes=["Qt"])
                for kt in range(4):
                    for jc in range(4):
                        b = (kt * 4 + jc) % 2
                        tr(pt[b][:, 0:128], K_F[:, kt, jc * 128:(jc + 1) * 128],
                           reads=["KF", "id"], writes=[("pt", b)])
                        for hp in range(2):
                            SE(lambda e, kt=kt, jc=jc, b=b, hp=hp:
                               e.activation(Kt_a[:, kt, jc, hp * 64:(hp + 1) * 64],
                                            pt[b][:, hp * 64:(hp + 1) * 64], AF.Copy,
                                            scale=kvec_s[:, 2 * kt + hp:2 * kt + hp + 1]),
                               reads=[("pt", b), "kvec"], writes=["Kt"])
                for which in range(2):
                    scol = 1024 + which * 1024
                    for half in range(2):
                        P.dma("sync", wTld[:, half, :, :],
                              d_wT[:, :, scol + half * 512:scol + (half + 1) * 512],
                              reads=["dwT"], writes=[("wT", half)])
                    for tc in range(NTC):
                        for half in range(2):
                            bkey = ("pb", half)
                            pv = pb[half]
                            coff = half * 512
                            for fc in range(8):
                                mm(pv[:, :], Xf[:, fc, (sub * 4 + tc) * 128:(sub * 4 + tc) * 128 + 128],
                                   wTld[:, half, fc, :],
                                   start=(fc == 0), stop=(fc == 7),
                                   reads=[("wT", half), "Xf"], writes=[bkey])
                            if which == 0:
                                VE(lambda e, pv=pv, tc=tc, coff=coff:
                                   e.tensor_copy(V_T[:, tc, coff:coff + 512], pv[:, :]),
                                   reads=[bkey], writes=["VT"])
                            else:
                                SE(lambda e, pv=pv, tc=tc, coff=coff:
                                   e.activation(G_T[:, tc, coff:coff + 512], pv[:, :], AF.Silu),
                                   reads=[bkey], writes=["GT"])
                P.maybe_barrier()
                if dbg and bel == 0 and blk == 0 and sub == 0:
                    P.dma("sync", d_tap["qf0"][:, :, :], Q_F[:, :, :], reads=["QF"])
                    P.dma("sync", d_tap["kf0"][:, :, :], K_F[:, :, :], reads=["KF"])
                    P.dma("sync", d_tap["vt0"][:, :, :], V_T[:, :, :], reads=["VT"])
                    P.dma("sync", d_tap["gt0"][:, :, :], G_T[:, :, :], reads=["GT"])

                for c in range(NTC):
                    cg = sub * 4 + c
                    cglob = blk * 8 + cg
                    for hh in range(H):
                        it = c * 8 + hh
                        cp = it % 2
                        qt, parh = hh // 2, hh % 2
                        pbase = parh * 64
                        mm(pb[cp][:, 0:128],
                           K_F[pbase:pbase + 64, qt, c * 128:(c + 1) * 128],
                           Q_F[pbase:pbase + 64, qt, c * 128:(c + 1) * 128],
                           True, True, reads=["KF", "QF"], writes=[("pb", cp)])
                        VE(lambda e, cp=cp, hh=hh: e.tensor_tensor(
                            ScT_s[:, cp, :], pb[cp][:, 0:128], msk_s[:, hh, :], OP.mult),
                           reads=[("pb", cp), "msk"], writes=[("ScT", cp)])
                        first = (cglob == 0)
                        mm(pb[2 + cp][:, 0:128], ScT_s[:, cp, :],
                           V_T[:, c, hh * 128:(hh + 1) * 128],
                           True, first, reads=[("ScT", cp), "VT"], writes=[("pb", 2 + cp)])
                        if not first:
                            mm(pb[2 + cp][:, 0:128],
                               Qt_a[pbase:pbase + 64, qt, c * 128:(c + 1) * 128],
                               S_bf[pbase:pbase + 64, qt, (cglob - 1) % 2, :],
                               False, True, reads=["Qt", f"Sbf{qt}"], writes=[("pb", 2 + cp)])
                        sbank = 4 + qt % 2
                        mm(pb[sbank][pbase:pbase + 64, 0:128],
                           Kt_a[:, qt, c, pbase:pbase + 64],
                           V_T[:, c, hh * 128:(hh + 1) * 128],
                           True, True, reads=["Kt", "VT"], writes=[("pb", sbank)])
                        if parh == 1:
                            gsc = 0.0 if cglob == 0 else g128_s[:, qt:qt + 1]
                            VE(lambda e, qt=qt, sbank=sbank, gsc=gsc: e.scalar_tensor_tensor(
                                S_sb[:, qt, :], S_sb[:, qt, :], gsc, pb[sbank][:, 0:128],
                                OP.mult, OP.add),
                               reads=[("pb", sbank), f"Ssb{qt}", "g128"],
                               writes=[f"Ssb{qt}"])
                            SE(lambda e, qt=qt, pp=cglob % 2: e.activation(
                                S_bf[:, qt, pp, :], S_sb[:, qt, :], AF.Copy),
                               reads=[f"Ssb{qt}"], writes=[f"Sbf{qt}"])
                        slot = it % NOR
                        SE(lambda e, slot=slot, cp=cp: e.activation(
                            Oraw[:, slot, :], pb[2 + cp][:, 0:128], AF.Copy),
                           reads=[("pb", 2 + cp)], writes=[("Or", slot)])
                        VE(lambda e, slot=slot: e.bn_stats(st6r[:, slot, :], Oraw[:, slot, :]),
                           reads=[("Or", slot)], writes=[("st", slot)])
                        VE(lambda e, slot=slot: e.bn_aggr(mvr[:, slot, :], st6r[:, slot, :]),
                           reads=[("st", slot)], writes=[("mv", slot)])
                        SE(lambda e, slot=slot: e.activation(negm_r[:, slot, :], mvr[:, slot, 0:1],
                                                             AF.Copy, scale=-1.0),
                           reads=[("mv", slot)], writes=[("ng", slot)])
                        SE(lambda e, slot=slot: e.activation(rstd_r[:, slot, :], mvr[:, slot, 1:2],
                                                             AF.Ln, bias=eps_v[:, :]),
                           reads=[("mv", slot), "eps"], writes=[("rs", slot)])
                        SE(lambda e, slot=slot: e.activation(rstd_r[:, slot, :], rstd_r[:, slot, :],
                                                             AF.Exp, scale=-0.5),
                           reads=[("rs", slot)], writes=[("rs", slot)])
                        SE(lambda e, slot=slot: e.activation(bcol_r[:, slot, :], negm_r[:, slot, :],
                                                             AF.Copy, scale=rstd_r[:, slot, :]),
                           reads=[("ng", slot), ("rs", slot)], writes=[("bc", slot)])
                        g3 = it % 3
                        VE(lambda e, slot=slot, g3=g3: e.tensor_scalar(
                            tg1[:, g3, :], Oraw[:, slot, :], rstd_r[:, slot, :],
                            bcol_r[:, slot, :], OP.mult, OP.add),
                           reads=[("Or", slot), ("rs", slot), ("bc", slot)], writes=[("tg1", g3)])
                        PO(lambda e, g3=g3, hh=hh: e.tensor_tensor(
                            tg2[:, g3, :], tg1[:, g3, :], gnw_b[:, hh * 128:(hh + 1) * 128], OP.mult),
                           reads=[("tg1", g3), "gnw"], writes=[("tg2", g3)])
                        PO(lambda e, g3=g3, hh=hh: e.tensor_tensor(
                            tg3[:, g3, :], tg2[:, g3, :], gnb_b[:, hh * 128:(hh + 1) * 128], OP.add),
                           reads=[("tg2", g3), "gnb"], writes=[("tg3", g3)])
                        VE(lambda e, g3=g3, hh=hh, c=c: e.tensor_tensor(
                            Og_T[:, c, hh * 128:(hh + 1) * 128], tg3[:, g3, :],
                            G_T[:, c, hh * 128:(hh + 1) * 128], OP.mult),
                           reads=[("tg3", g3), "GT"], writes=["OgT"])
                P.maybe_barrier()
                if dbg and bel == 0 and blk == 0 and sub == 0:
                    P.dma("sync", d_tap["ogt0"][:, :, :], Og_T[:, :, :], reads=["OgT"])

                for half in range(2):
                    P.dma("sync", wTld[:, half, :, :],
                          d_wT[:, :, 3072 + half * 512:3072 + (half + 1) * 512],
                          reads=["dwT"], writes=[("wT", half)])
                for tc in range(NTC):
                    tcg = sub * 4 + tc
                    sl = tc % 2
                    for hc in range(8):
                        b = (tc * 8 + hc) % 2
                        tr(pt[b][:, 0:128], Og_T[:, tc, hc * 128:(hc + 1) * 128],
                           reads=["OgT", "id"], writes=[("pt", b)])
                        SE(lambda e, hc=hc, sl=sl, b=b:
                           e.activation(Og_F[:, sl, hc, :], pt[b][:, 0:128], AF.Copy),
                           reads=[("pt", b)], writes=[("OgF", sl)])
                    for dmh in range(2):
                        bsel = (tc * 2 + dmh) % 2
                        bkey = ("pb", bsel)
                        pw = pb[bsel]
                        for hc in range(8):
                            mm(pw[:, :], Og_F[:, sl, hc, :],
                               wTld[:, dmh, hc, :],
                               start=(hc == 0), stop=(hc == 7),
                               reads=[("OgF", sl), ("wT", dmh)], writes=[bkey])
                        VE(lambda e, pw=pw, tcg=tcg, dmh=dmh: e.tensor_copy(
                            Oacc[:, tcg, dmh * 512:dmh * 512 + 512], pw[:, :]),
                           reads=[bkey], writes=[("Oacc", tcg)])
                P.maybe_barrier()

            if dbg and bel == 0 and blk == 0:
                P.dma("sync", d_tap["xf0"][:, :, :], Xf[:, :, :], reads=["Xf"])
                P.dma("sync", d_tap["wblk0"][:, :, :], wblk[:, :, :], reads=["wblk"])
                P.dma("sync", d_tap["oattn"][:, :, :], Oacc[:, :, :], reads=[("Oacc", t) for t in range(8)])

            for e_i in range(NE):
                for rq in range(4):
                    load_w2(e_i, rq)
                    for hc in range(8):
                        for th in range(2):
                            bkey = ("pb", (hc * 2 + th) % 2)
                            ph = pb[(hc * 2 + th) % 2]
                            for fc in range(8):
                                mm(ph[:, :], w1buf[:, fc, hc * 128:(hc + 1) * 128],
                                   Xf[:, fc, th * 512:th * 512 + 512],
                                   start=(fc == 0), stop=(fc == 7),
                                   reads=["w1", "Xf"], writes=[bkey])
                            col = e_i * 32 + rq * 8 + hc
                            SE(lambda e, ph=ph, hc=hc, th=th, col=col:
                               e.activation(H_sb[:, hc, th * 512:th * 512 + 512], ph[:, :],
                                            AF.Gelu, bias=b1_s[:, col:col + 1]),
                               reads=[bkey, "b1"], writes=["H"])
                    if dbg and bel == 0 and blk == 0 and e_i == 0 and rq == 0:
                        P.dma("sync", d_tap["hsb0"][:, :, :], H_sb[:, :, :], reads=["H"])
                    if rq < 3:
                        load_w1(e_i, rq + 1)
                    elif e_i + 1 < NE:
                        load_w1(e_i + 1, 0)
                    last = (rq == 3)
                    for tc in range(8):
                        for dmh in range(2):
                            bsel = 2 + (tc * 2 + dmh) % 2
                            bkey = ("pb", bsel)
                            pO = pb[bsel]
                            for hc in range(8):
                                is_end = (hc == 7) and not last
                                mm(pO[:, :], H_sb[:, hc, tc * 128:(tc + 1) * 128],
                                   w2buf[:, hc, dmh * 512:dmh * 512 + 512],
                                   start=(hc == 0), stop=is_end,
                                   reads=["H", "w2"], writes=[bkey])
                            if last:
                                mm(pO[:, :], oh_s[:, e_i * 128:(e_i + 1) * 128],
                                   b2_s[:, dmh * 512:dmh * 512 + 512],
                                   False, True, reads=["oh", "b2s"], writes=[bkey])
                            VE(lambda e, pO=pO, tc=tc, dmh=dmh, e_i=e_i: e.scalar_tensor_tensor(
                                Oacc[:, tc, dmh * 512:dmh * 512 + 512], pO[:, :],
                                wblk[:, tc, e_i:e_i + 1], Oacc[:, tc, dmh * 512:dmh * 512 + 512],
                                OP.mult, OP.add),
                               reads=[bkey, "wblk", ("Oacc", tc)], writes=[("Oacc", tc)])
                    P.maybe_barrier()

            if dbg and bel == 0 and blk == 0:
                P.dma("sync", d_tap["ofin"][:, :, :], Oacc[:, :, :], reads=[("Oacc", t) for t in range(8)])

            for tcg in range(8):
                par = tcg % 2
                toff = blk_t0 + tcg * 128
                P.dma("sync", x_t[:, par, :], d_x[bel, toff:toff + 128, :],
                      writes=[("xt", par)])
                VE(lambda e, par=par, tcg=tcg: e.tensor_tensor(
                    Oacc[:, tcg, :], Oacc[:, tcg, :], x_t[:, par, :], OP.add),
                   reads=[("Oacc", tcg), ("xt", par)], writes=[("Oacc", tcg)])
                P.dma("sync", d_y[bel, toff:toff + 128, :], Oacc[:, tcg, :],
                      reads=[("Oacc", tcg)], writes=["y"])
            P.maybe_barrier()

    P.barrier()

    with nc.Block() as block:
        @block.sync
        def _(eng):
            for fn in P.q["sync"]:
                fn(eng)

        @block.gpsimd
        def _(eng):
            for fn in P.q["gpsimd"]:
                fn(eng)

        @block.vector
        def _(eng):
            for fn in P.q["vector"]:
                fn(eng)

        @block.scalar
        def _(eng):
            for fn in P.q["scalar"]:
                fn(eng)

        @block.tensor
        def _(eng):
            for fn in P.q["tensor"]:
                fn(eng)

    es.close()
    return nc


def kernel(L0_attn_x, L0_attn_Wq, L0_attn_Wk, L0_attn_Wv, L0_attn_Wg, L0_attn_Wo,
           L0_attn_gn_weight, L0_attn_gn_bias, L0_ffn_router_w, L0_ffn_router_b,
           L0_ffn_lang_w, L0_ffn_lang_b, L0_ffn_w1, L0_ffn_b1, L0_ffn_w2, L0_ffn_b2):
    consts = build_consts()
    fixed = dict(
        wq=np.ascontiguousarray(L0_attn_Wq, np.float32),
        wk=np.ascontiguousarray(L0_attn_Wk, np.float32),
        wv=np.ascontiguousarray(L0_attn_Wv, np.float32),
        wg=np.ascontiguousarray(L0_attn_Wg, np.float32),
        wo=np.ascontiguousarray(L0_attn_Wo, np.float32),
        gnw=np.ascontiguousarray(L0_attn_gn_weight, np.float32),
        gnb=np.ascontiguousarray(L0_attn_gn_bias, np.float32),
        rw=np.ascontiguousarray(L0_ffn_router_w, np.float32),
        rb=np.ascontiguousarray(L0_ffn_router_b, np.float32),
        lw=np.ascontiguousarray(L0_ffn_lang_w, np.float32),
        lb=np.ascontiguousarray(L0_ffn_lang_b, np.float32),
        w1=np.ascontiguousarray(L0_ffn_w1, np.float32),
        b1=np.ascontiguousarray(L0_ffn_b1, np.float32),
        w2=np.ascontiguousarray(L0_ffn_w2, np.float32),
        b2=np.ascontiguousarray(L0_ffn_b2, np.float32),
        rope_cos=consts["rope_cos"], rope_sin=consts["rope_sin"],
        masks=consts["masks"], qsc=consts["qsc"], kvec=consts["kvec"],
        g128c=consts["g128c"], p2t=consts["p2t"], ident=consts["ident"],
        onehot=consts["onehot"],
    )
    x = np.ascontiguousarray(L0_attn_x, np.float32)
    in_maps = []
    for c in range(NCORES):
        m = dict(fixed)
        m["x"] = x[c * BPC:(c + 1) * BPC]
        in_maps.append(m)
    nc = build_nc()
    res = run_bass_kernel_spmd(nc, in_maps, core_ids=list(range(NCORES)))
    return np.concatenate([np.asarray(r["y"], np.float32) for r in res.results], axis=0)


# version 08 (final)
# speedup vs baseline: 1.2679x; 1.2679x over previous
import math
import numpy as np
import ml_dtypes
from contextlib import ExitStack

import concourse.bass as bass
import concourse.mybir as mybir
from concourse.bass_utils import run_bass_kernel_spmd

F32 = mybir.dt.float32
BF16 = mybir.dt.bfloat16
AF = mybir.ActivationFunctionType
OP = mybir.AluOpType
AX = mybir.AxisListType

NCORES = 8
B_TOT, T, D = 16, 2048, 1024
BPC = B_TOT // NCORES
H, HD, HV = 8, 64, 128
NE, EH, NL = 8, 4096, 4
NBLK = 2
NSUB = 2
NTC = 4
CH = 128
EPS = 1e-5
DSC = 1.0 / math.sqrt(HD)

BARRIER = False


def build_consts():
    gam = 1.0 - np.exp2(-5.0 - np.arange(H, dtype=np.float64))
    invf = 1.0 / (10000.0 ** (np.arange(0, HD, 2, dtype=np.float64) / HD))
    t_ = np.arange(T, dtype=np.float64)
    d_ = np.arange(128) % HD
    th = t_[None, :] * invf[d_ % 32][:, None]
    rope_cos = np.cos(th).astype(np.float32)
    rope_sin = np.sin(th).astype(np.float32)
    j = np.arange(CH)
    rel = j[None, :] - j[:, None]
    masks = np.zeros((CH, H, CH), np.float32)
    for h in range(H):
        masks[:, h, :] = (np.where(rel >= 0, gam[h] ** np.maximum(rel, 0), 0.0) * DSC).astype(np.float32)
    qsc = np.zeros((128, H // 2, CH), np.float32)
    for qt in range(H // 2):
        for par in range(2):
            g = gam[2 * qt + par]
            qsc[par * 64:(par + 1) * 64, qt, :] = (g ** (np.arange(CH) + 1.0) * DSC)[None, :]
    kvec = np.zeros((128, H), np.float32)
    for h in range(H):
        kvec[:, h] = gam[h] ** (127.0 - np.arange(128))
    g128c = np.zeros((128, H // 2), np.float32)
    for qt in range(H // 2):
        g128c[0:64, qt] = gam[2 * qt] ** 128
        g128c[64:128, qt] = gam[2 * qt + 1] ** 128
    p2 = np.zeros((128, 128), np.float32)
    for b in range(2):
        for dd in range(32):
            p2[b * 64 + dd, b * 64 + dd + 32] = -1.0
            p2[b * 64 + dd + 32, b * 64 + dd] = 1.0
    p2t = np.asarray(p2.T, dtype=ml_dtypes.bfloat16)
    masks = np.asarray(masks, dtype=ml_dtypes.bfloat16)
    qsc = np.asarray(qsc, dtype=ml_dtypes.bfloat16)
    ident = np.asarray(np.eye(128), dtype=ml_dtypes.bfloat16)
    onehot = np.zeros((8, 1024), np.float32)
    for e in range(8):
        onehot[e, e * 128:(e + 1) * 128] = 1.0
    onehot = np.asarray(onehot, dtype=ml_dtypes.bfloat16)
    return dict(rope_cos=rope_cos, rope_sin=rope_sin, masks=masks, qsc=qsc,
                kvec=kvec, g128c=g128c, p2t=p2t, ident=ident, onehot=onehot)


class Sem:
    def __init__(self, handle, name):
        self.h = handle
        self.name = name
        self.n = 0

    def inc(self, k=1):
        self.n += k
        return self.n


class Prog:
    ENGS = ("sync", "gpsimd", "vector", "scalar", "tensor")
    CTAG = {"tensor": "TE", "vector": "VE", "scalar": "SE", "gpsimd": "PO"}
    QTAG = {"sync": "SQ", "gpsimd": "GQ"}

    def __init__(self, nc, es):
        self.q = {e: [] for e in self.ENGS}
        self.sems = {n: Sem(es.enter_context(nc.semaphore(n)), n)
                     for n in ("TE", "VE", "SE", "PO", "SQ", "GQ")}
        self.hw = {}
        self.res = {}

    def _wait(self, eng, semname, val):
        if val <= 0:
            return
        key = (eng, semname)
        if self.hw.get(key, 0) >= val:
            return
        self.hw[key] = val
        s = self.sems[semname].h
        self.q[eng].append(lambda e, s=s, v=val: e.wait_ge(s, v))

    def _edges(self, eng, tag, reads, writes):
        for k in reads:
            rec = self.res.get(k)
            if rec:
                for sn, v in rec["w"].items():
                    self._wait(eng, sn, v)
        for k in writes:
            rec = self.res.get(k)
            if rec:
                for sn, v in rec["w"].items():
                    if sn != tag:
                        self._wait(eng, sn, v)
                for sn, v in rec["r"].items():
                    if sn != tag:
                        self._wait(eng, sn, v)

    def _record(self, tag, val, reads, writes):
        for k in writes:
            rec = self.res.setdefault(k, {"w": {}, "r": {}})
            if rec["w"].get(tag, 0) < val:
                rec["w"][tag] = val
        for k in reads:
            rec = self.res.setdefault(k, {"w": {}, "r": {}})
            if rec["r"].get(tag, 0) < val:
                rec["r"][tag] = val

    def op(self, eng, fn, reads=(), writes=()):
        tag = self.CTAG[eng]
        self._edges(eng, tag, reads, writes)
        sem = self.sems[tag]
        v = sem.inc(1)
        self.q[eng].append(lambda e, f=fn, s=sem.h: f(e).then_inc(s, 1))
        self._record(tag, v, reads, writes)
        return (tag, v)

    def dma(self, qeng, out, in_, reads=(), writes=(), slow=False):
        tag = self.QTAG[qeng]
        self._edges(qeng, tag, reads, writes)
        sem = self.sems[tag]
        v = sem.inc(16)
        if slow:
            self.q[qeng].append(lambda e, o=out, i=in_, s=sem.h: e.dma_start(
                out=o, in_=i, allow_slow_non_contiguous=True).then_inc(s, 16))
        else:
            self.q[qeng].append(lambda e, o=out, i=in_, s=sem.h: e.dma_start(
                out=o, in_=i).then_inc(s, 16))
        self._record(tag, v, reads, writes)
        return (tag, v)

    def barrier(self):
        snap = [(n, s.n) for n, s in self.sems.items()]
        for eng in self.ENGS:
            for n, v in snap:
                self._wait(eng, n, v)

    def maybe_barrier(self):
        if BARRIER:
            self.barrier()


def build_nc(dbg=False):
    consts = build_consts()

    nc = bass.Bass("TRN2", target_bir_lowering=False)
    d_x = nc.dram_tensor("x", [BPC, T, D], F32, kind="ExternalInput").ap()
    d_wq = nc.dram_tensor("wq", [H * HD, D], F32, kind="ExternalInput").ap()
    d_wk = nc.dram_tensor("wk", [H * HD, D], F32, kind="ExternalInput").ap()
    d_wv = nc.dram_tensor("wv", [H * HV, D], F32, kind="ExternalInput").ap()
    d_wg = nc.dram_tensor("wg", [H * HV, D], F32, kind="ExternalInput").ap()
    d_wo = nc.dram_tensor("wo", [D, H * HV], F32, kind="ExternalInput").ap()
    d_gnw = nc.dram_tensor("gnw", [H * HV], F32, kind="ExternalInput").ap()
    d_gnb = nc.dram_tensor("gnb", [H * HV], F32, kind="ExternalInput").ap()
    d_rw = nc.dram_tensor("rw", [NE, D], F32, kind="ExternalInput").ap()
    d_rb = nc.dram_tensor("rb", [NE], F32, kind="ExternalInput").ap()
    d_lw = nc.dram_tensor("lw", [NL, D], F32, kind="ExternalInput").ap()
    d_lb = nc.dram_tensor("lb", [NL], F32, kind="ExternalInput").ap()
    d_w1 = nc.dram_tensor("w1", [NE, D, EH], F32, kind="ExternalInput").ap()
    d_b1 = nc.dram_tensor("b1", [NE, EH], F32, kind="ExternalInput").ap()
    d_w2 = nc.dram_tensor("w2", [NE, EH, D], F32, kind="ExternalInput").ap()
    d_b2 = nc.dram_tensor("b2", [NE, D], F32, kind="ExternalInput").ap()
    d_cos = nc.dram_tensor("rope_cos", [128, T], F32, kind="ExternalInput").ap()
    d_sin = nc.dram_tensor("rope_sin", [128, T], F32, kind="ExternalInput").ap()
    d_msk = nc.dram_tensor("masks", [CH, H, CH], BF16, kind="ExternalInput").ap()
    d_qsc = nc.dram_tensor("qsc", [128, H // 2, CH], BF16, kind="ExternalInput").ap()
    d_kvec = nc.dram_tensor("kvec", [128, H], F32, kind="ExternalInput").ap()
    d_g128 = nc.dram_tensor("g128c", [128, H // 2], F32, kind="ExternalInput").ap()
    d_p2t = nc.dram_tensor("p2t", [128, 128], BF16, kind="ExternalInput").ap()
    d_id = nc.dram_tensor("ident", [128, 128], BF16, kind="ExternalInput").ap()
    d_oh = nc.dram_tensor("onehot", [8, 1024], BF16, kind="ExternalInput").ap()
    d_y = nc.dram_tensor("y", [BPC, T, D], F32, kind="ExternalOutput").ap()
    if dbg:
        d_tap = {
            "xf0": nc.dram_tensor("t_xf0", [128, 8, 1024], BF16, kind="ExternalOutput").ap(),
            "wblk0": nc.dram_tensor("t_wblk0", [128, 8, 8], F32, kind="ExternalOutput").ap(),
            "qf0": nc.dram_tensor("t_qf0", [128, 4, 512], BF16, kind="ExternalOutput").ap(),
            "kf0": nc.dram_tensor("t_kf0", [128, 4, 512], BF16, kind="ExternalOutput").ap(),
            "vt0": nc.dram_tensor("t_vt0", [128, 4, 1024], BF16, kind="ExternalOutput").ap(),
            "gt0": nc.dram_tensor("t_gt0", [128, 4, 1024], BF16, kind="ExternalOutput").ap(),
            "ogt0": nc.dram_tensor("t_ogt0", [128, 4, 1024], BF16, kind="ExternalOutput").ap(),
            "oattn": nc.dram_tensor("t_oattn", [8, 128, 1024], BF16, kind="ExternalOutput").ap(),
            "hsb0": nc.dram_tensor("t_hsb0", [128, 8, 1024], BF16, kind="ExternalOutput").ap(),
            "ofin": nc.dram_tensor("t_ofin", [128, 8, 1024], F32, kind="ExternalOutput").ap(),
        }
    d_wT = nc.dram_tensor("wT_scr", [128, 8, 4096], BF16, kind="Internal").ap()
    d_oat = nc.dram_tensor("oat_scr", [8, 128, 1024], BF16, kind="Internal").ap()

    es = ExitStack()

    def sb(name, shape, dt):
        return es.enter_context(nc.sbuf_tensor(name, shape, dt))

    def ps(name, shape, dt):
        return es.enter_context(nc.psum_tensor(name, shape, dt))

    Xf = sb("Xf", [128, 8, 1024], BF16)
    Oacc = sb("Oacc", [128, 8, 1024], F32)
    wblk = sb("wblk", [128, 8, NE], F32)
    msk_s = sb("msk_s", [128, H, CH], BF16)
    qsc_s = sb("qsc_s", [128, H // 2, CH], BF16)
    kvec_s = sb("kvec_s", [128, H], F32)
    g128_s = sb("g128_s", [128, H // 2], F32)
    p2t_s = sb("p2t_s", [128, 128], BF16)
    id_s = sb("id_s", [128, 128], BF16)
    gnw_b = sb("gnw_b", [128, H * HV], BF16)
    gnb_b = sb("gnb_b", [128, H * HV], BF16)
    rlwT = sb("rlwT", [128, 8, 16], BF16)
    rlb_b = sb("rlb_b", [128, 16], F32)
    b1_s = sb("b1_s", [128, NE * 32], F32)
    b2_s = sb("b2_s", [8, 1024], BF16)
    oh_s = sb("oh_s", [8, 1024], BF16)
    eps_v = sb("eps_v", [128, 1], F32)
    cos_s = sb("cos_s", [128, 512], F32)
    sin_s = sb("sin_s", [128, 512], F32)
    S_sb = sb("S_sb", [128, H // 2, 128], F32)
    S_bf = sb("S_bf", [128, H // 2, 2, 128], BF16)
    x_t = sb("x_t", [128, 2, 1024], F32)
    xn_s = sb("xn_s", [128, 2, 1024], BF16)
    lst = sb("lst", [128, 2, 16], F32)
    rt_sc = sb("rt_sc", [128, 2, 64], F32)
    Qraw = sb("Qraw", [128, 2, 512], BF16)
    rot1 = sb("rot1", [128, 512], BF16)
    rot2 = sb("rot2", [128, 512], BF16)
    Q_F = sb("Q_F", [128, 4, 512], BF16)
    K_F = sb("K_F", [128, 4, 512], BF16)
    Qt_a = sb("Qt_a", [128, 4, 512], BF16)
    Kt_a = sb("Kt_a", [128, 4, 4, 128], BF16)
    V_T = sb("V_T", [128, 4, 1024], BF16)
    G_T = sb("G_T", [128, 4, 1024], BF16)
    Og_T = sb("Og_T", [128, 4, 1024], BF16)
    Og_F = sb("Og_F", [128, 4, 8, 128], BF16)
    ScT_s = sb("ScT_s", [128, 2, 128], BF16)
    NOR = 5
    NTG = 2
    Oraw = sb("Oraw", [128, NOR, 128], F32)
    st6r = sb("st6r", [128, NOR, 6], F32)
    mvr = sb("mvr", [128, NOR, 2], F32)
    negm_r = sb("negm_r", [128, NOR, 1], F32)
    rstd_r = sb("rstd_r", [128, NOR, 1], F32)
    bcol_r = sb("bcol_r", [128, NOR, 1], F32)
    tg1 = sb("tg1", [128, NTG, 128], BF16)
    tg2 = sb("tg2", [128, NTG, 128], BF16)
    tg3 = sb("tg3", [128, NTG, 128], BF16)
    wTld = sb("wTld", [128, 2, 8, 512], BF16)
    Oas = sb("Oas", [128, 2, 512], BF16)

    pb = [ps(f"pb{i}", [128, 512], F32) for i in range(6)]
    pt = [ps(f"pt{i}", [128, 1024], BF16) for i in range(2)]

    P = Prog(nc, es)

    nums = sorted(s.h.num for s in P.sems.values())
    assert nums[-1] - nums[0] == len(nums) - 1, nums
    nc.all_engine_barrier()
    nc.gpsimd.dma_reset(range(nums[0], nums[-1] + 1))
    nc.gpsimd.sem_clear(range(nums[0], nums[-1] + 1))
    nc.all_engine_barrier()

    sink = [None]

    def _route(eng, fn, reads, writes):
        if sink[0] is not None:
            sink[0].append(lambda eng=eng, fn=fn, r=tuple(reads), w=tuple(writes):
                           P.op(eng, fn, r, w))
            return None
        return P.op(eng, fn, reads, writes)

    def TE(fn, reads=(), writes=()):
        return _route("tensor", fn, reads, writes)

    def VE(fn, reads=(), writes=()):
        return _route("vector", fn, reads, writes)

    def SE(fn, reads=(), writes=()):
        return _route("scalar", fn, reads, writes)

    def PO(fn, reads=(), writes=()):
        return _route("gpsimd", fn, reads, writes)

    def DMA(qeng, out, in_, reads=(), writes=(), slow=False):
        if sink[0] is not None:
            sink[0].append(lambda qeng=qeng, out=out, in_=in_, r=tuple(reads),
                           w=tuple(writes), slow=slow: P.dma(qeng, out, in_, r, w, slow))
            return None
        return P.dma(qeng, out, in_, reads, writes, slow)

    def bkeys(bsel):
        return [("pb", 4)] if bsel == 4 else [("pb5o",), ("pb5s",)]

    def mm(out_ap, lhsT, rhs, start, stop, reads=(), writes=()):
        return TE(lambda t, o=out_ap, l=lhsT, r=rhs, st=start, sp=stop:
                  t.matmul(o, l, r, start=st, stop=sp), reads, writes)

    def tr(out_ap, in_ap, reads=(), writes=(), ident_ap=None):
        ia = ident_ap if ident_ap is not None else id_s[:, :]
        return TE(lambda t, o=out_ap, i=in_ap, ia=ia: t.transpose(o, i, ia),
                  reads, writes)

    for out, in_, key in [
            (msk_s[:, :, :], d_msk[:, :, :], "msk"),
            (qsc_s[:, :, :], d_qsc[:, :, :], "qsc"),
            (kvec_s[:, :], d_kvec[:, :], "kvec"),
            (g128_s[:, :], d_g128[:, :], "g128"),
            (p2t_s[:, :], d_p2t[:, :], "p2t"),
            (id_s[:, :], d_id[:, :], "id"),
            (rlb_b[0:1, 0:8], d_rb[None, :], "rlb"),
            (rlb_b[0:1, 8:12], d_lb[None, :], "rlb"),
            (oh_s[:, :], d_oh[:, :], "oh")]:
        P.dma("sync", out, in_, writes=[key])
    P.dma("sync", b1_s[:, :], d_b1.rearrange("e (hcg p) -> p (e hcg)", p=128),
          writes=["b1"], slow=True)
    P.dma("gpsimd", b2_s[:, :], d_b2[:, :], writes=["b2s"])

    VE(lambda e: e.memset(eps_v[:, :], EPS), writes=["eps"])
    VE(lambda e: e.memset(S_sb.rearrange("p a b -> p (a b)"), 0.0), writes=["Ssb0", "Ssb1", "Ssb2", "Ssb3"])
    VE(lambda e: e.memset(S_bf.rearrange("p a b c -> p (a b c)"), 0.0), writes=["Sbf0", "Sbf1", "Sbf2", "Sbf3"])
    VE(lambda e: e.memset(rlb_b[0:1, 12:16], 0.0), writes=["rlb"])

    with nc.sbuf_tensor("wtmp", [128, 4, 1024], BF16) as wtmp, \
         nc.sbuf_tensor("wTst", [128, 8, 512], BF16) as wTst, \
         nc.sbuf_tensor("gn_f32", [1, 2, H * HV], F32) as gn_f32, \
         nc.sbuf_tensor("onescol", [1, 128], F32) as onescol, \
         nc.sbuf_tensor("rl_s", [12, 1024], F32) as rl_s, \
         nc.sbuf_tensor("rl_bf", [12, 1024], BF16) as rl_bf:
        P.dma("sync", gn_f32[0:1, 0, :], d_gnw[None, :], writes=["gnf"])
        P.dma("sync", gn_f32[0:1, 1, :], d_gnb[None, :], writes=["gnf"])
        VE(lambda e: e.memset(onescol[:, :], 1.0), writes=["ones"])
        for gsel in range(2):
            dstb = gnw_b if gsel == 0 else gnb_b
            dkey = "gnw" if gsel == 0 else "gnb"
            for half in range(2):
                b = (gsel * 2 + half) % 2
                mm(pb[b][:, :], onescol[:, :],
                   gn_f32[0:1, gsel, half * 512:(half + 1) * 512],
                   True, True, reads=["ones", "gnf"], writes=[("pb", b)])
                VE(lambda e, dstb=dstb, half=half, b=b: e.tensor_copy(
                    dstb[:, half * 512:(half + 1) * 512], pb[b][:, :]),
                    reads=[("pb", b)], writes=[dkey])
        mm(pb[2][:, 0:16], onescol[:, :], rlb_b[0:1, :], True, True,
           reads=["ones", "rlb"], writes=[("pb", 2)])
        VE(lambda e: e.tensor_copy(rlb_b[:, :], pb[2][:, 0:16]),
           reads=[("pb", 2)], writes=["rlb"])
        P.dma("sync", rl_s[0:8, :], d_rw[:, :], writes=["rls"])
        P.dma("sync", rl_s[8:12, :], d_lw[:, :], writes=["rls"])
        VE(lambda e: e.tensor_copy(rl_bf[:, :], rl_s[:, :]),
           reads=["rls"], writes=["rlbf"])
        for fc in range(8):
            b = fc % 2
            tr(pt[b][:, 0:12], rl_bf[:, fc * 128:(fc + 1) * 128],
               reads=["rlbf", "id"], writes=[("pt", b)],
               ident_ap=id_s[0:12, 0:12])
            SE(lambda e, fc=fc, b=b: e.activation(rlwT[:, fc, 0:12], pt[b][:, 0:12], AF.Copy),
               reads=[("pt", b)], writes=["rlwT"])
        VE(lambda e: e.memset(rlwT[:, :, 12:16], 0.0), writes=["rlwT"])

        specs = [(d_wq, 512, 0), (d_wk, 512, 512), (d_wv, 1024, 1024),
                 (d_wg, 1024, 2048), (d_wo, 1024, 3072)]
        for w_d, rows, coff in specs:
            for hp in range(rows // 512):
                P.dma("gpsimd", wtmp[:, :, :],
                      w_d[hp * 512:(hp + 1) * 512, :].rearrange("(oc p) f -> p oc f", p=128),
                      writes=["wtmp"])
                for oc in range(4):
                    for fc in range(8):
                        b = (oc * 8 + fc) % 2
                        tr(pt[b][:, 0:128], wtmp[:, oc, fc * 128:(fc + 1) * 128],
                           reads=["wtmp", "id"], writes=[("pt", b)])
                        SE(lambda e, fc=fc, oc=oc, b=b:
                           e.activation(wTst[:, fc, oc * 128:(oc + 1) * 128], pt[b][:, 0:128], AF.Copy),
                           reads=[("pt", b)], writes=["wTst"])
                P.dma("sync", d_wT[:, :, coff + hp * 512:coff + (hp + 1) * 512],
                      wTst[:, :, :], reads=["wTst"], writes=["dwT"])
        P.barrier()

    w1buf = sb("w1buf", [128, 8, 1024], BF16)
    w2buf = sb("w2buf", [128, 8, 1024], BF16)
    H_sb = sb("H_sb", [128, 8, 1024], BF16)

    def load_w1(e_i, rq):
        return P.dma("gpsimd", w1buf[:, :, :],
                      d_w1[e_i, :, rq * 1024:(rq + 1) * 1024].rearrange("(fc p) h -> p fc h", p=128),
                      writes=["w1"])

    def load_w2(e_i, rq):
        return P.dma("gpsimd", w2buf[:, :, :],
                      d_w2[e_i, rq * 1024:(rq + 1) * 1024, :].rearrange("(hc p) d -> p hc d", p=128),
                      writes=["w2"])

    from collections import deque

    for bel in range(BPC):
        for blk in range(NBLK):
            blk_t0 = blk * 1024
            load_w1(0, 0)

            for tcg in range(8):
                par = tcg % 2
                toff = blk_t0 + tcg * 128
                xk = ("xt", par)
                lk = ("lst", par)
                P.dma("sync", x_t[:, par, :], d_x[bel, toff:toff + 128, :], writes=[xk])
                VE(lambda e, par=par: e.bn_stats(lst[:, par, 0:6], x_t[:, par, 0:512]),
                   reads=[xk], writes=[lk])
                VE(lambda e, par=par: e.bn_stats(lst[:, par, 6:12], x_t[:, par, 512:1024]),
                   reads=[xk], writes=[lk])
                VE(lambda e, par=par: e.bn_aggr(lst[:, par, 12:14], lst[:, par, 0:12]),
                   reads=[lk], writes=[lk])
                SE(lambda e, par=par: e.activation(lst[:, par, 14:15], lst[:, par, 12:13],
                                                   AF.Copy, scale=-1.0),
                   reads=[lk], writes=[lk])
                SE(lambda e, par=par: e.activation(lst[:, par, 15:16], lst[:, par, 13:14],
                                                   AF.Ln, bias=eps_v[:, :]),
                   reads=[lk, "eps"], writes=[lk])
                SE(lambda e, par=par: e.activation(lst[:, par, 15:16], lst[:, par, 15:16],
                                                   AF.Exp, scale=-0.5),
                   reads=[lk], writes=[lk])
                VE(lambda e, par=par: e.tensor_scalar(
                    xn_s[:, par, :], x_t[:, par, :], lst[:, par, 14:15],
                    lst[:, par, 15:16], OP.add, OP.mult),
                   reads=[xk, lk], writes=[("xn", par)])
                for fc in range(8):
                    b = fc % 2
                    tr(pt[b][:, 0:128], xn_s[:, par, fc * 128:(fc + 1) * 128],
                       reads=[("xn", par), "id"], writes=[("pt", b)])
                    SE(lambda e, fc=fc, b=b, off=tcg * 128:
                       e.activation(Xf[:, fc, off:off + 128], pt[b][:, 0:128], AF.Copy),
                       reads=[("pt", b)], writes=["Xf"])

            for tcg in range(8):
                sl = tcg % 2
                bkey = bkeys(4 + sl)
                pr = pb[4 + sl]
                rk = ("rt", sl)
                r = rt_sc[:, sl, :]
                for fc in range(8):
                    mm(pr[:, 0:16], Xf[:, fc, tcg * 128:tcg * 128 + 128],
                       rlwT[:, fc, :], start=(fc == 0), stop=(fc == 7),
                       reads=["Xf", "rlwT"], writes=bkey)
                VE(lambda e, pr=pr, r=r: e.tensor_tensor(r[:, 0:16], pr[:, 0:16], rlb_b[:, :], OP.add),
                   reads=bkey + ["rlb"], writes=[rk])
                VE(lambda e, r=r: e.reduce_max(r[:, 44:45], r[:, 8:12], axis=AX.X, negate=True),
                   reads=[rk], writes=[rk])
                SE(lambda e, r=r: e.activation(r[:, 16:20], r[:, 8:12], AF.Exp, bias=r[:, 44:45]),
                   reads=[rk], writes=[rk])
                VE(lambda e, r=r: e.reduce_sum(r[:, 45:46], r[:, 16:20], axis=AX.X),
                   reads=[rk], writes=[rk])
                VE(lambda e, r=r: e.reciprocal(r[:, 46:47], r[:, 45:46]),
                   reads=[rk], writes=[rk])
                VE(lambda e, r=r: e.tensor_scalar(r[:, 20:24], r[:, 16:20], r[:, 46:47], None, OP.mult),
                   reads=[rk], writes=[rk])
                VE(lambda e, r=r: e.tensor_tensor(r[:, 24:27], r[:, 20:23], r[:, 21:24], OP.add),
                   reads=[rk], writes=[rk])
                VE(lambda e, r=r: e.tensor_copy(r[:, 27:28], r[:, 23:24]),
                   reads=[rk], writes=[rk])
                VE(lambda e, r=r: e.scalar_tensor_tensor(r[:, 28:32], r[:, 24:28], 0.1, r[:, 0:4], OP.mult, OP.add),
                   reads=[rk], writes=[rk])
                VE(lambda e, r=r: e.scalar_tensor_tensor(r[:, 32:36], r[:, 24:28], 0.1, r[:, 4:8], OP.mult, OP.add),
                   reads=[rk], writes=[rk])
                VE(lambda e, r=r: e.max(r[:, 36:44], r[:, 28:36]),
                   reads=[rk], writes=[rk])
                VE(lambda e, r=r: e.tensor_scalar(r[:, 44:45], r[:, 36:37], -1.0, None, OP.mult),
                   reads=[rk], writes=[rk])
                SE(lambda e, r=r: e.activation(r[:, 48:56], r[:, 28:36], AF.Exp, bias=r[:, 44:45]),
                   reads=[rk], writes=[rk])
                SE(lambda e, r=r: e.activation(r[:, 45:46], r[:, 37:38], AF.Exp, bias=r[:, 44:45]),
                   reads=[rk], writes=[rk])
                VE(lambda e, r=r: e.tensor_scalar(r[:, 46:47], r[:, 45:46], 1.0, None, OP.add),
                   reads=[rk], writes=[rk])
                VE(lambda e, r=r: e.reciprocal(r[:, 47:48], r[:, 46:47]),
                   reads=[rk], writes=[rk])
                VE(lambda e, r=r: e.tensor_scalar(r[:, 56:64], r[:, 28:36], r[:, 37:38], None, OP.is_ge),
                   reads=[rk], writes=[rk])
                VE(lambda e, r=r, tcg=tcg: e.scalar_tensor_tensor(
                    wblk[:, tcg, :], r[:, 48:56], r[:, 47:48], r[:, 56:64], OP.mult, OP.mult),
                   reads=[rk], writes=["wblk"])

            A = []
            sink[0] = A
            for sub in range(NSUB):
                sub_t0 = blk_t0 + sub * 512
                DMA("sync", cos_s[:, :], d_cos[:, sub_t0:sub_t0 + 512], writes=["cos"])
                DMA("sync", sin_s[:, :], d_sin[:, sub_t0:sub_t0 + 512], writes=["sin"])

                wcols = [0, 512, 1024, 1536, 2048, 2560, 3072, 3584]

                def wload(g):
                    DMA("sync", wTld[:, g % 2, :, :], d_wT[:, :, wcols[g]:wcols[g] + 512],
                        reads=["dwT"], writes=[("wT", g % 2)])
                wload(0)
                wload(1)
                for grp in range(2):
                    for tile in range(4):
                        bsel = 4 + tile % 2
                        pq = pb[bsel]
                        for fc in range(8):
                            mm(pq[:, :], wTld[:, grp, fc, tile * 128:(tile + 1) * 128],
                               Xf[:, fc, sub * 512:sub * 512 + 512],
                               start=(fc == 0), stop=(fc == 7),
                               reads=[("wT", grp), "Xf"], writes=bkeys(bsel))
                        qsl = tile % 2
                        SE(lambda e, pq=pq, qsl=qsl: e.activation(Qraw[:, qsl, :], pq[:, :], AF.Copy),
                           reads=bkeys(bsel), writes=[("Qraw", qsl)])
                        pksel = 4 + (tile + 1) % 2
                        prot = pb[pksel]
                        mm(prot[:, :], p2t_s[:, :], Qraw[:, qsl, :], True, True,
                           reads=["p2t", ("Qraw", qsl)], writes=bkeys(pksel))
                        VE(lambda e, qsl=qsl: e.tensor_tensor(rot1[:, :], Qraw[:, qsl, :], cos_s[:, :], OP.mult),
                           reads=[("Qraw", qsl), "cos"], writes=["rot1"])
                        VE(lambda e, prot=prot: e.tensor_tensor(rot2[:, :], prot[:, :], sin_s[:, :], OP.mult),
                           reads=bkeys(pksel) + ["sin"], writes=["rot2"])
                        dst = Q_F if grp == 0 else K_F
                        dk = "QF" if grp == 0 else "KF"
                        VE(lambda e, dst=dst, dt_i=tile: e.tensor_tensor(
                            dst[:, dt_i, :], rot1[:, :], rot2[:, :], OP.add),
                           reads=["rot1", "rot2"], writes=[dk])
                    wload(grp + 2)
                for qt in range(4):
                    for c in range(4):
                        VE(lambda e, qt=qt, c=c: e.tensor_tensor(
                            Qt_a[:, qt, c * 128:(c + 1) * 128],
                            Q_F[:, qt, c * 128:(c + 1) * 128],
                            qsc_s[:, qt, :], OP.mult),
                           reads=["QF", "qsc"], writes=["Qt"])
                for kt in range(4):
                    for jc in range(4):
                        b = (kt * 4 + jc) % 2
                        tr(pt[b][:, 0:128], K_F[:, kt, jc * 128:(jc + 1) * 128],
                           reads=["KF", "id"], writes=[("pt", b)])
                        for hp in range(2):
                            SE(lambda e, kt=kt, jc=jc, b=b, hp=hp:
                               e.activation(Kt_a[:, kt, jc, hp * 64:(hp + 1) * 64],
                                            pt[b][:, hp * 64:(hp + 1) * 64], AF.Copy,
                                            scale=kvec_s[:, 2 * kt + hp:2 * kt + hp + 1]),
                               reads=[("pt", b), "kvec"], writes=["Kt"])
                for g in range(2, 6):
                    which = (g - 2) // 2
                    coff = ((g - 2) % 2) * 512
                    for tc in range(NTC):
                        bsel = 4 + tc % 2
                        pv = pb[bsel]
                        for fc in range(8):
                            mm(pv[:, :], Xf[:, fc, (sub * 4 + tc) * 128:(sub * 4 + tc) * 128 + 128],
                               wTld[:, g % 2, fc, :],
                               start=(fc == 0), stop=(fc == 7),
                               reads=[("wT", g % 2), "Xf"], writes=bkeys(bsel))
                        if which == 0:
                            VE(lambda e, pv=pv, tc=tc, coff=coff:
                               e.tensor_copy(V_T[:, tc, coff:coff + 512], pv[:, :]),
                               reads=bkeys(bsel), writes=["VT"])
                        else:
                            SE(lambda e, pv=pv, tc=tc, coff=coff:
                               e.activation(G_T[:, tc, coff:coff + 512], pv[:, :], AF.Silu),
                               reads=bkeys(bsel), writes=["GT"])
                    if g + 2 < 8:
                        wload(g + 2)

                def stageA(it):
                    c, hh = it // 8, it % 8
                    qt, parh = hh // 2, hh % 2
                    pbase = parh * 64
                    cp = it % 2
                    mm(pb[4][:, 0:128],
                       K_F[pbase:pbase + 64, qt, c * 128:(c + 1) * 128],
                       Q_F[pbase:pbase + 64, qt, c * 128:(c + 1) * 128],
                       True, True, reads=["KF", "QF"], writes=[("pb", 4)])
                    VE(lambda e, cp=cp, hh=hh: e.tensor_tensor(
                        ScT_s[:, cp, :], pb[4][:, 0:128], msk_s[:, hh, :], OP.mult),
                       reads=[("pb", 4), "msk"], writes=[("ScT", cp)])

                def stageB(it, cglob0):
                    c, hh = it // 8, it % 8
                    cglob = cglob0 + c
                    qt, parh = hh // 2, hh % 2
                    pbase = parh * 64
                    cp = it % 2
                    first = (cglob == 0)
                    mm(pb[5][:, 0:128], ScT_s[:, cp, :],
                       V_T[:, c, hh * 128:(hh + 1) * 128],
                       True, first, reads=[("ScT", cp), "VT"], writes=[("pb5o",)])
                    if not first:
                        mm(pb[5][:, 0:128],
                           Qt_a[pbase:pbase + 64, qt, c * 128:(c + 1) * 128],
                           S_bf[pbase:pbase + 64, qt, (cglob - 1) % 2, :],
                           False, True, reads=["Qt", f"Sbf{qt}"], writes=[("pb5o",)])
                    mm(pb[5][pbase:pbase + 64, 128:256],
                       Kt_a[:, qt, c, pbase:pbase + 64],
                       V_T[:, c, hh * 128:(hh + 1) * 128],
                       True, True, reads=["Kt", "VT"], writes=[("pb5s",)])
                    gsc = 0.0 if cglob == 0 else g128_s[pbase:pbase + 64, qt:qt + 1]
                    VE(lambda e, qt=qt, pbase=pbase, gsc=gsc: e.scalar_tensor_tensor(
                        S_sb[pbase:pbase + 64, qt, :], S_sb[pbase:pbase + 64, qt, :],
                        gsc, pb[5][pbase:pbase + 64, 128:256], OP.mult, OP.add),
                       reads=[("pb5s",), f"Ssb{qt}", "g128"], writes=[f"Ssb{qt}"])
                    if parh == 1:
                        SE(lambda e, qt=qt, pp=cglob % 2: e.activation(
                            S_bf[:, qt, pp, :], S_sb[:, qt, :], AF.Copy),
                           reads=[f"Ssb{qt}"], writes=[f"Sbf{qt}"])
                    slot = it % NOR
                    SE(lambda e, slot=slot: e.activation(
                        Oraw[:, slot, :], pb[5][:, 0:128], AF.Copy),
                       reads=[("pb5o",)], writes=[("Or", slot)])

                def stageC(it):
                    c, hh = it // 8, it % 8
                    slot = it % NOR
                    VE(lambda e, slot=slot: e.bn_stats(st6r[:, slot, :], Oraw[:, slot, :]),
                       reads=[("Or", slot)], writes=[("st", slot)])
                    VE(lambda e, slot=slot: e.bn_aggr(mvr[:, slot, :], st6r[:, slot, :]),
                       reads=[("st", slot)], writes=[("mv", slot)])
                    SE(lambda e, slot=slot: e.activation(negm_r[:, slot, :], mvr[:, slot, 0:1],
                                                         AF.Copy, scale=-1.0),
                       reads=[("mv", slot)], writes=[("ng", slot)])
                    SE(lambda e, slot=slot: e.activation(rstd_r[:, slot, :], mvr[:, slot, 1:2],
                                                         AF.Ln, bias=eps_v[:, :]),
                       reads=[("mv", slot), "eps"], writes=[("rs", slot)])
                    SE(lambda e, slot=slot: e.activation(rstd_r[:, slot, :], rstd_r[:, slot, :],
                                                         AF.Exp, scale=-0.5),
                       reads=[("rs", slot)], writes=[("rs", slot)])
                    SE(lambda e, slot=slot: e.activation(bcol_r[:, slot, :], negm_r[:, slot, :],
                                                         AF.Copy, scale=rstd_r[:, slot, :]),
                       reads=[("ng", slot), ("rs", slot)], writes=[("bc", slot)])
                    g3 = it % NTG
                    VE(lambda e, slot=slot, g3=g3: e.tensor_scalar(
                        tg1[:, g3, :], Oraw[:, slot, :], rstd_r[:, slot, :],
                        bcol_r[:, slot, :], OP.mult, OP.add),
                       reads=[("Or", slot), ("rs", slot), ("bc", slot)], writes=[("tg1", g3)])
                    PO(lambda e, g3=g3, hh=hh: e.tensor_tensor(
                        tg2[:, g3, :], tg1[:, g3, :], gnw_b[:, hh * 128:(hh + 1) * 128], OP.mult),
                       reads=[("tg1", g3), "gnw"], writes=[("tg2", g3)])
                    PO(lambda e, g3=g3, hh=hh: e.tensor_tensor(
                        tg3[:, g3, :], tg2[:, g3, :], gnb_b[:, hh * 128:(hh + 1) * 128], OP.add),
                       reads=[("tg2", g3), "gnb"], writes=[("tg3", g3)])
                    VE(lambda e, g3=g3, hh=hh, c=c: e.tensor_tensor(
                        Og_T[:, c, hh * 128:(hh + 1) * 128], tg3[:, g3, :],
                        G_T[:, c, hh * 128:(hh + 1) * 128], OP.mult),
                       reads=[("tg3", g3), "GT"], writes=["OgT"])

                cglob0 = blk * 8 + sub * 4
                for cyc in range(34):
                    if 1 <= cyc <= 32:
                        stageB(cyc - 1, cglob0)
                    if cyc < 32:
                        stageA(cyc)
                    if cyc >= 2:
                        stageC(cyc - 2)

                for tc in range(NTC):
                    for hc in range(8):
                        b = (tc * 8 + hc) % 2
                        tr(pt[b][:, 0:128], Og_T[:, tc, hc * 128:(hc + 1) * 128],
                           reads=["OgT", "id"], writes=[("pt", b)])
                        SE(lambda e, hc=hc, tc=tc, b=b:
                           e.activation(Og_F[:, tc, hc, :], pt[b][:, 0:128], AF.Copy),
                           reads=[("pt", b)], writes=[("OgF", tc)])
                for dmh in range(2):
                    for tc in range(NTC):
                        tcg = sub * 4 + tc
                        bsel = 4 + tc % 2
                        pw = pb[bsel]
                        for hc in range(8):
                            mm(pw[:, :], Og_F[:, tc, hc, :],
                               wTld[:, dmh, hc, :],
                               start=(hc == 0), stop=(hc == 7),
                               reads=[("OgF", tc), ("wT", dmh)], writes=bkeys(bsel))
                        osl = tc % 2
                        VE(lambda e, pw=pw, osl=osl: e.tensor_copy(
                            Oas[:, osl, :], pw[:, :]),
                           reads=bkeys(bsel), writes=[("Oas", osl)])
                        DMA("sync", d_oat[tcg, :, dmh * 512:dmh * 512 + 512],
                            Oas[:, osl, :], reads=[("Oas", osl)], writes=["dOat"])
            sink[0] = None
            Aq = deque(A)

            def pace(n=2):
                for _ in range(n):
                    if Aq:
                        Aq.popleft()()

            for e_i in range(NE):
                for rq in range(4):
                    load_w2(e_i, rq)
                    for hc in range(8):
                        for th in range(2):
                            bkey = ("pb", (hc * 2 + th) % 2)
                            ph = pb[(hc * 2 + th) % 2]
                            for fc in range(8):
                                mm(ph[:, :], w1buf[:, fc, hc * 128:(hc + 1) * 128],
                                   Xf[:, fc, th * 512:th * 512 + 512],
                                   start=(fc == 0), stop=(fc == 7),
                                   reads=["w1", "Xf"], writes=[bkey])
                            col = e_i * 32 + rq * 8 + hc
                            SE(lambda e, ph=ph, hc=hc, th=th, col=col:
                               e.activation(H_sb[:, hc, th * 512:th * 512 + 512], ph[:, :],
                                            AF.Gelu, bias=b1_s[:, col:col + 1]),
                               reads=[bkey, "b1"], writes=["H"])
                            pace()
                    if rq < 3:
                        load_w1(e_i, rq + 1)
                    elif e_i + 1 < NE:
                        load_w1(e_i + 1, 0)
                    last = (rq == 3)
                    for tc in range(8):
                        for dmh in range(2):
                            bsel = 2 + (tc * 2 + dmh) % 2
                            bkey = ("pb", bsel)
                            pO = pb[bsel]
                            for hc in range(8):
                                is_end = (hc == 7) and not last
                                mm(pO[:, :], H_sb[:, hc, tc * 128:(tc + 1) * 128],
                                   w2buf[:, hc, dmh * 512:dmh * 512 + 512],
                                   start=(hc == 0), stop=is_end,
                                   reads=["H", "w2"], writes=[bkey])
                            if last:
                                mm(pO[:, :], oh_s[:, e_i * 128:(e_i + 1) * 128],
                                   b2_s[:, dmh * 512:dmh * 512 + 512],
                                   False, True, reads=["oh", "b2s"], writes=[bkey])
                            if e_i == 0 and rq == 0:
                                VE(lambda e, pO=pO, tc=tc, dmh=dmh: e.tensor_scalar(
                                    Oacc[:, tc, dmh * 512:dmh * 512 + 512], pO[:, :],
                                    wblk[:, tc, 0:1], None, OP.mult),
                                   reads=[bkey, "wblk"], writes=[("Oacc", tc)])
                            else:
                                VE(lambda e, pO=pO, tc=tc, dmh=dmh, e_i=e_i: e.scalar_tensor_tensor(
                                    Oacc[:, tc, dmh * 512:dmh * 512 + 512], pO[:, :],
                                    wblk[:, tc, e_i:e_i + 1], Oacc[:, tc, dmh * 512:dmh * 512 + 512],
                                    OP.mult, OP.add),
                                   reads=[bkey, "wblk", ("Oacc", tc)], writes=[("Oacc", tc)])
                            pace()
            while Aq:
                Aq.popleft()()

            if dbg and bel == 0 and blk == 0:
                P.dma("sync", d_tap["xf0"][:, :, :], Xf[:, :, :], reads=["Xf"])
                P.dma("sync", d_tap["wblk0"][:, :, :], wblk[:, :, :], reads=["wblk"])
                P.dma("sync", d_tap["oattn"][:, :, :], d_oat[:, :, :], reads=["dOat"])
                P.dma("sync", d_tap["ofin"][:, :, :], Oacc[:, :, :], reads=[("Oacc", t) for t in range(8)])

            for tcg in range(8):
                par = tcg % 2
                toff = blk_t0 + tcg * 128
                P.dma("sync", x_t[:, par, :], d_x[bel, toff:toff + 128, :],
                      writes=[("xt", par)])
                P.dma("sync", Qraw[:, 0, :], d_oat[tcg, :, 0:512],
                      reads=["dOat"], writes=[("Qraw", 0)])
                P.dma("sync", Qraw[:, 1, :], d_oat[tcg, :, 512:1024],
                      reads=["dOat"], writes=[("Qraw", 1)])
                VE(lambda e, tcg=tcg: e.scalar_tensor_tensor(
                    Oacc[:, tcg, :], Qraw.rearrange("p a b -> p (a b)"), 1.0,
                    Oacc[:, tcg, :], OP.mult, OP.add),
                   reads=[("Qraw", 0), ("Qraw", 1), ("Oacc", tcg)], writes=[("Oacc", tcg)])
                VE(lambda e, par=par, tcg=tcg: e.tensor_tensor(
                    Oacc[:, tcg, :], Oacc[:, tcg, :], x_t[:, par, :], OP.add),
                   reads=[("Oacc", tcg), ("xt", par)], writes=[("Oacc", tcg)])
                P.dma("sync", d_y[bel, toff:toff + 128, :], Oacc[:, tcg, :],
                      reads=[("Oacc", tcg)], writes=["y"])

    P.barrier()

    with nc.Block() as block:
        @block.sync
        def _(eng):
            for fn in P.q["sync"]:
                fn(eng)

        @block.gpsimd
        def _(eng):
            for fn in P.q["gpsimd"]:
                fn(eng)

        @block.vector
        def _(eng):
            for fn in P.q["vector"]:
                fn(eng)

        @block.scalar
        def _(eng):
            for fn in P.q["scalar"]:
                fn(eng)

        @block.tensor
        def _(eng):
            for fn in P.q["tensor"]:
                fn(eng)

    es.close()
    return nc


def kernel(L0_attn_x, L0_attn_Wq, L0_attn_Wk, L0_attn_Wv, L0_attn_Wg, L0_attn_Wo,
           L0_attn_gn_weight, L0_attn_gn_bias, L0_ffn_router_w, L0_ffn_router_b,
           L0_ffn_lang_w, L0_ffn_lang_b, L0_ffn_w1, L0_ffn_b1, L0_ffn_w2, L0_ffn_b2):
    consts = build_consts()
    fixed = dict(
        wq=np.ascontiguousarray(L0_attn_Wq, np.float32),
        wk=np.ascontiguousarray(L0_attn_Wk, np.float32),
        wv=np.ascontiguousarray(L0_attn_Wv, np.float32),
        wg=np.ascontiguousarray(L0_attn_Wg, np.float32),
        wo=np.ascontiguousarray(L0_attn_Wo, np.float32),
        gnw=np.ascontiguousarray(L0_attn_gn_weight, np.float32),
        gnb=np.ascontiguousarray(L0_attn_gn_bias, np.float32),
        rw=np.ascontiguousarray(L0_ffn_router_w, np.float32),
        rb=np.ascontiguousarray(L0_ffn_router_b, np.float32),
        lw=np.ascontiguousarray(L0_ffn_lang_w, np.float32),
        lb=np.ascontiguousarray(L0_ffn_lang_b, np.float32),
        w1=np.ascontiguousarray(L0_ffn_w1, np.float32),
        b1=np.ascontiguousarray(L0_ffn_b1, np.float32),
        w2=np.ascontiguousarray(L0_ffn_w2, np.float32),
        b2=np.ascontiguousarray(L0_ffn_b2, np.float32),
        rope_cos=consts["rope_cos"], rope_sin=consts["rope_sin"],
        masks=consts["masks"], qsc=consts["qsc"], kvec=consts["kvec"],
        g128c=consts["g128c"], p2t=consts["p2t"], ident=consts["ident"],
        onehot=consts["onehot"],
    )
    x = np.ascontiguousarray(L0_attn_x, np.float32)
    in_maps = []
    for c in range(NCORES):
        m = dict(fixed)
        m["x"] = x[c * BPC:(c + 1) * BPC]
        in_maps.append(m)
    nc = build_nc()
    res = run_bass_kernel_spmd(nc, in_maps, core_ids=list(range(NCORES)))
    return np.concatenate([np.asarray(r["y"], np.float32) for r in res.results], axis=0)
